# version 02
# baseline (speedup 1.0000x reference)
import bisect
import math
from contextlib import ExitStack

import numpy as np
import ml_dtypes
import concourse.bass as bass
import concourse.mybir as mybir
from concourse.bass_utils import run_bass_kernel_spmd

F32 = mybir.dt.float32
BF16 = mybir.dt.bfloat16
AF = mybir.ActivationFunctionType
ALU = mybir.AluOpType
AX = mybir.AxisListType

S = 4096
D = 1024
NCH = 8
TB = 512
NTB = S // TB
NH = 16
DH = 64
NEG0 = -1.0e30
NEG1 = -1.0e9
WIN = 512

BFNP = np.dtype(ml_dtypes.bfloat16)



def host_consts():
    c = {}
    c["c_ident"] = np.eye(128, dtype=np.float32)
    inv = (1.0 / 10000.0 ** (np.arange(0, DH, 2, dtype=np.float32) / DH))
    t = np.arange(S, dtype=np.float32)
    fr = np.outer(inv, t)
    cosq = np.cos(fr).astype(np.float32)
    sinq = np.sin(fr).astype(np.float32)
    c["c_cos"] = np.tile(cosq, (4, 1))
    c["c_sin"] = np.concatenate([-sinq, sinq, -sinq, sinq], 0)
    ii = np.arange(128)
    jj = np.arange(128)
    JI = jj[None, :] - ii[:, None]
    c["c_mask_up30_T"] = np.where(JI > 0, NEG0, 0.0).astype(np.float32).T.copy()
    js = np.arange(WIN)
    dslab = ((js[None, :] - WIN) - ii[:, None]).astype(np.float32)
    mslab = np.zeros((128, WIN), dtype=np.float32)
    mslab[:, :128] += np.where(JI < 0, NEG1, 0.0)
    c["c_dslab"] = dslab
    c["c_mslab"] = mslab
    c["c_ddiag"] = JI.astype(np.float32)
    c["c_mdiag"] = np.where(JI > 0, NEG1, 0.0).astype(np.float32)
    E8 = np.zeros((8, 8 * 128), dtype=np.float32)
    for e in range(8):
        E8[e, e * 128:(e + 1) * 128] = 1.0
    c["c_E8"] = E8.astype(BFNP)
    E16 = np.zeros((16, 16 * 128), dtype=np.float32)
    for e in range(16):
        E16[e, e * 128:(e + 1) * 128] = 1.0
    c["c_E16"] = E16.astype(BFNP)
    c["c_slopes"] = np.asarray([2.0 ** (-8.0 / NH * (h + 1)) for h in range(NH)],
                               np.float32).reshape(NH, 1)
    return c


def prep_core_inputs(b, inputs, consts):
    m = dict(consts)
    m["x_fm"] = np.ascontiguousarray(np.asarray(inputs["L0_attn_x"][b]).T)
    for nm in ("Wq", "Wk", "Wv", "Wo"):
        m[f"w0_{nm}"] = np.ascontiguousarray(np.asarray(inputs[f"L0_attn_{nm}"]).T)
    perm = np.concatenate([np.arange(32, 64), np.arange(0, 32)])
    permD = np.concatenate([64 * h + perm for h in range(NH)])
    m["w0_Wq_sw"] = np.ascontiguousarray(m["w0_Wq"][:, permD])
    m["w0_Wk_sw"] = np.ascontiguousarray(m["w0_Wk"][:, permD])
    m["g0_scale"] = np.asarray(inputs["L0_ffn_ln_scale"])
    m["g0_bias"] = np.asarray(inputs["L0_ffn_ln_bias"])
    m["g0_gwT"] = np.ascontiguousarray(np.asarray(inputs["L0_ffn_gate_w"]).T)
    for nm in ("Wg", "Wu", "Wd", "sWg", "sWu", "sWd"):
        m[f"f0_{nm}"] = np.asarray(inputs[f"L0_ffn_{nm}"])
    for nm in ("wq", "wk", "wv", "wo"):
        m[f"w1_{nm}"] = np.ascontiguousarray(np.asarray(inputs[f"L1_attn_{nm}"]).T)
    m["w1_qg"] = np.asarray(inputs["L1_attn_q_gamma"]).reshape(1, D)
    m["w1_kg"] = np.asarray(inputs["L1_attn_k_gamma"]).reshape(1, D)
    m["g1_centT"] = np.ascontiguousarray(np.asarray(inputs["L1_ffn_centroids"]).T)
    m["g1_eb"] = np.asarray(inputs["L1_ffn_expert_biases"]).reshape(16, 1)
    m["f1_WrT"] = np.ascontiguousarray(np.swapaxes(np.asarray(inputs["L1_ffn_Wr"]), 1, 2))
    m["f1_br"] = np.asarray(inputs["L1_ffn_br"])
    m["f1_WsT"] = np.ascontiguousarray(np.swapaxes(np.asarray(inputs["L1_ffn_Ws"]), 1, 2))
    m["f1_bs"] = np.asarray(inputs["L1_ffn_bs"])
    return m



ENGINES = ("tensor", "vector", "scalar", "gpsimd", "sync")
DMA_Q = {"sync": "qy", "gpsimd": "qg", "scalar": "qa"}


class _Dom:

    def __init__(self, stack, nc, name, step):
        self.h = stack.enter_context(nc.semaphore("sem_" + name))
        self.step = step
        self.seqs = []
        self.vals = []
        self.nval = 0

    def ensure(self, op):
        i = bisect.bisect_left(self.seqs, op.seq)
        if i < len(self.seqs):
            return self.vals[i]
        self.nval += self.step
        op.inc = (self.h, self.step)
        self.seqs.append(op.seq)
        self.vals.append(self.nval)
        return self.nval


class Opr:
    __slots__ = ("eng", "dom", "seq", "fn", "waits", "inc")

    def __init__(self, eng, dom, seq, fn):
        self.eng, self.dom, self.seq, self.fn = eng, dom, seq, fn
        self.waits = []
        self.inc = None


class Sched:
    def __init__(self, nc, stack):
        self.nc = nc
        self.doms = {e: _Dom(stack, nc, e, 1) for e in ENGINES}
        for q in ("qy", "qg", "qa"):
            self.doms[q] = _Dom(stack, nc, q, 16)
        self.ops = {e: [] for e in ENGINES}
        self.seq = {e: 0 for e in ENGINES}
        self.writer = {}
        self.readers = {}
        self.waited = {e: {} for e in ENGINES}
        self.flushed = False

    def _dep(self, op, prod):
        if prod is None:
            return
        if prod.dom == op.dom:
            if prod.dom == "tensor" or prod.dom in ("qy", "qg", "qa"):
                return
        d = self.doms[prod.dom]
        v = d.ensure(prod)
        w = self.waited[op.eng]
        if w.get(prod.dom, -1) >= v:
            return
        w[prod.dom] = v
        op.waits.append((d.h, v))

    def op(self, eng, fn, reads=(), writes=(), dma=False):
        dom = DMA_Q[eng] if dma else eng
        o = Opr(eng, dom, self.seq[eng], fn)
        self.seq[eng] += 1
        for t in reads:
            self._dep(o, self.writer.get(t))
        for t in writes:
            self._dep(o, self.writer.get(t))
            for r in self.readers.get(t, {}).values():
                self._dep(o, r)
        for t in reads:
            self.readers.setdefault(t, {})[o.dom] = o
        for t in writes:
            self.writer[t] = o
            self.readers[t] = {}
        if dma:
            self.doms[dom].ensure(o)
        self.ops[eng].append(o)
        return o

    def dma_flushed(self, fn, reads=(), writes=()):
        o1 = self.op("scalar", fn, reads, (), dma=True)
        d = self.doms["qa"]
        v = d.ensure(o1)
        o2 = self.op("scalar", lambda e: e.nop(), (), writes)
        w = self.waited["scalar"]
        if w.get("qa", -1) < v:
            w["qa"] = v
            o2.waits.append((d.h, v))
        return o2

    def barrier(self):
        marks = []
        for d in ("tensor", "vector", "scalar", "gpsimd", "sync", "qy", "qg", "qa"):
            dd = self.doms[d]
            ops = self.ops.get(d) if d in ENGINES else None
            if d in ENGINES:
                last = self.ops[d][-1] if self.ops[d] else None
                last = None
                for cand in reversed(self.ops[d]):
                    if cand.dom == d:
                        last = cand
                        break
                if last is not None:
                    marks.append((d, dd.ensure(last)))
            else:
                eng = {"qy": "sync", "qg": "gpsimd", "qa": "scalar"}[d]
                last = None
                for cand in reversed(self.ops[eng]):
                    if cand.dom == d:
                        last = cand
                        break
                if last is not None:
                    marks.append((d, dd.ensure(last)))
        for e in ENGINES:
            def f(x):
                return x.nop()
            o = Opr(e, e, self.seq[e], f)
            self.seq[e] += 1
            w = self.waited[e]
            for dname, v in marks:
                if w.get(dname, -1) < v:
                    w[dname] = v
                    o.waits.append((self.doms[dname].h, v))
            self.ops[e].append(o)
        self.writer.clear()
        self.readers.clear()

    def flush(self):
        assert not self.flushed
        self.flushed = True
        nc = self.nc
        with nc.Block("main") as blk:
            for e in ENGINES:
                ops = self.ops[e]

                def runner(engobj, ops=ops):
                    for o in ops:
                        for (h, v) in o.waits:
                            engobj.wait_ge(h, v)
                        inst = o.fn(engobj)
                        if o.inc is not None:
                            inst.then_inc(o.inc[0], o.inc[1])
                getattr(blk, e)(runner)


class ByteArena:

    def __init__(self, stack, nc, name, cols_bf16):
        self.t = stack.enter_context(nc.sbuf_tensor(name, [128, cols_bf16], BF16))
        self.cols = cols_bf16
        self.top = 0
        self.resv = cols_bf16
        self.name = name
        self.n = 0
        self.hiwater = 0

    def _view(self, c0, cols, dtype):
        ap = self.t[:, c0:c0 + cols]
        if dtype != BF16:
            ap = ap.bitcast(dtype)
        return ap

    def alloc(self, nelems, dtype, align=32):
        cols = nelems * (2 if dtype == F32 else 1)
        o = (self.top + align - 1) // align * align
        assert o + cols <= self.resv, f"{self.name} overflow: {o}+{cols} > {self.resv}"
        self.top = o + cols
        self.hiwater = max(self.hiwater, self.top)
        self.n += 1
        return self._view(o, cols, dtype), (self.name, "t", self.n)

    def alloc_p(self, nelems, dtype, align=32):
        cols = nelems * (2 if dtype == F32 else 1)
        r = (self.resv - cols) // align * align
        assert r >= self.top, f"{self.name} overflow (persistent)"
        self.resv = r
        self.n += 1
        return self._view(r, cols, dtype), (self.name, "p", self.n)

    def reset(self):
        self.top = 0


class DtView:
    def __init__(self, arena, dtype):
        self.a = arena
        self.dt = dtype

    def alloc(self, n, align=32):
        return self.a.alloc(n, self.dt, align)

    def alloc_p(self, n, align=32):
        return self.a.alloc_p(n, self.dt, align)


class Ctx:
    def __init__(self, nc, stack):
        self.nc = nc
        self.stack = stack
        self.dram = {}
        self.sc = Sched(nc, stack)
        self.arena = ByteArena(stack, nc, "arena", 104448)
        self.ab = DtView(self.arena, BF16)
        self.af = DtView(self.arena, F32)
        self.pb = [stack.enter_context(nc.psum_tensor(f"pb{i}", [128, 512], F32))
                   for i in range(8)]
        self.pbt = [("psum", i) for i in range(8)]
        self.pt = [self.pb[6][:, :].bitcast(BF16), self.pb[7][:, :].bitcast(BF16)]
        self.ptt = [("psum", 6), ("psum", 7)]

    def dt(self, name, shape, dtype, kind):
        ap = self.nc.dram_tensor(name, list(shape), dtype, kind=kind).ap()
        self.dram[name] = ap
        return ap

    def phase(self):
        self.sc.barrier()
        self.arena.reset()


def fmv(ap, t0, t1):
    return ap[:, t0:t1].rearrange("(c p) t -> p c t", p=128)



def _mk(ctx):
    sc = ctx.sc
    ctx.t = lambda fn, r=(), w=(): sc.op("tensor", fn, r, w)
    ctx.v = lambda fn, r=(), w=(): sc.op("vector", fn, r, w)
    ctx.s = lambda fn, r=(), w=(): sc.op("scalar", fn, r, w)
    ctx.g = lambda fn, r=(), w=(): sc.op("gpsimd", fn, r, w)
    ctx.y = lambda fn, r=(), w=(): sc.op("sync", fn, r, w)
    ctx.dy = lambda fn, r=(), w=(): sc.op("sync", fn, r, w, dma=True)
    ctx.dg = lambda fn, r=(), w=(): sc.op("gpsimd", fn, r, w, dma=True)
    ctx.da = lambda fn, r=(), w=(): sc.op("scalar", fn, r, w, dma=True)
    ctx.das = lambda fn, r=(), w=(): sc.dma_flushed(fn, r, w)



ACT_CONSTS = [1e-5, 1e-6, -0.5, 0.125, 0.5, -1.0]


def emit_init(ctx):
    ab, af, d = ctx.ab, ctx.af, ctx.dram
    cblk, tok_cblk = af.alloc_p(len(ACT_CONSTS))
    for i, val in enumerate(ACT_CONSTS):
        ctx.g(lambda e, i=i, val=val: e.memset(cblk[:, i:i + 1], val), w=[tok_cblk])
        ctx.nc.const_aps.register(F32, val, cblk[:, i:i + 1])
    ctx.ident_b, ctx.tok_idb = ab.alloc_p(128)
    ctx.ident_f, ctx.tok_idf = af.alloc_p(128)
    ctx.mask0T_b, ctx.tok_m0 = ab.alloc_p(128)
    ctx.ones_col_f, ctx.tok_ocf = af.alloc_p(16)
    ctx.ones_col_b, ctx.tok_ocb = ab.alloc_p(32)
    ctx.ones_row_f, ctx.tok_orf = af.alloc_p(128)
    ctx.w_em0, ctx.tok_wem0 = ab.alloc_p(S)
    ctx.g_em1, ctx.tok_gem1 = ab.alloc_p(S)
    ctx.dy(lambda e: e.dma_start(out=ctx.ident_f, in_=d["c_ident"]), w=[ctx.tok_idf])
    ctx.dg(lambda e: e.dma_start(out=ctx.ident_b, in_=d["c_ident"]), w=[ctx.tok_idb])
    ctx.dg(lambda e: e.dma_start(out=ctx.mask0T_b, in_=d["c_mask_up30_T"]), w=[ctx.tok_m0])
    ctx.g(lambda e: e.memset(ctx.ones_col_f[:, 0:1], 1.0), w=[ctx.tok_ocf])
    ctx.g(lambda e: e.memset(ctx.ones_col_b[:, 0:1], 1.0), w=[ctx.tok_ocb])
    ctx.g(lambda e: e.memset(ctx.ones_row_f[0:1, :], 1.0), w=[ctx.tok_orf])
    ctx.phase()


def emit_gate_consts(ctx, layer):
    ab, af, d = ctx.ab, ctx.af, ctx.dram
    if layer == 1:
        cent_f, tok_cent = af.alloc(16 * NCH)
        eb_f, tok_eb = af.alloc(16)
        ctx.das(lambda e: e.dma_start(out=eb_f[0:16, 0:1], in_=d["g1_eb"]), w=[tok_eb])
        ctx.dg(lambda e: e.dma_start(out=cent_f.rearrange("p (c n) -> p c n", c=NCH),
                                     in_=d["g1_centT"].rearrange("(c p) n -> p c n", p=128)),
               w=[tok_cent])
        return dict(cent_f=cent_f, tok_cent=tok_cent, eb_f=eb_f, tok_eb=tok_eb)
    g0b_f, tok_g0b = af.alloc(NCH)
    g0s_f, tok_g0s = af.alloc(NCH)
    gws_f, tok_gws = af.alloc(8 * NCH)
    negc1_f, tok_nc1 = af.alloc(16)
    c2b_f, tok_c2b = af.alloc(16)
    gwT_f, tok_gwt = af.alloc(NCH * 8)
    gb_f, tok_gb = af.alloc(NCH * 8)
    c2row_f, tok_c2r = af.alloc(16)
    ctx.dy(lambda e: e.dma_start(out=g0b_f, in_=d["g0_bias"].rearrange("(c p) -> p c", p=128), allow_slow_non_contiguous=True), w=[tok_g0b])
    ctx.dy(lambda e: e.dma_start(out=g0s_f, in_=d["g0_scale"].rearrange("(c p) -> p c", p=128), allow_slow_non_contiguous=True), w=[tok_g0s])
    ctx.dg(lambda e: e.dma_start(out=gwT_f.rearrange("p (c n) -> p c n", c=NCH),
                                 in_=d["g0_gwT"].rearrange("(c p) n -> p c n", p=128)), w=[tok_gwt])
    for c in range(NCH):
        ctx.v(lambda e, c=c: e.tensor_scalar_mul(gws_f[:, c * 8:(c + 1) * 8],
                                                 gwT_f[:, c * 8:(c + 1) * 8],
                                                 g0s_f[:, c:c + 1]),
              r=[tok_gwt, tok_g0s], w=[tok_gws])
        ctx.v(lambda e, c=c: e.tensor_scalar_mul(gb_f[:, c * 8:(c + 1) * 8],
                                                 gwT_f[:, c * 8:(c + 1) * 8],
                                                 g0b_f[:, c:c + 1]),
              r=[tok_gwt, tok_g0b], w=[tok_gb])
    for c in range(NCH):
        ctx.t(lambda e, c=c: e.matmul(ctx.pb[0][0:1, 0:8], ctx.ones_col_f[:, 0:1],
                                      gws_f[:, c * 8:(c + 1) * 8],
                                      start=(c == 0), stop=(c == NCH - 1)),
              r=[ctx.tok_ocf, tok_gws], w=[ctx.pbt[0]])
    for c in range(NCH):
        ctx.t(lambda e, c=c: e.matmul(ctx.pb[1][0:1, 0:8], ctx.ones_col_f[:, 0:1],
                                      gb_f[:, c * 8:(c + 1) * 8],
                                      start=(c == 0), stop=(c == NCH - 1)),
              r=[ctx.tok_ocf, tok_gb], w=[ctx.pbt[1]])
    ctx.v(lambda e: e.tensor_scalar_mul(negc1_f[0:1, 0:8], ctx.pb[0][0:1, 0:8], -1.0),
          r=[ctx.pbt[0]], w=[tok_nc1])
    ctx.v(lambda e: e.tensor_copy(c2row_f[0:1, 0:8], ctx.pb[1][0:1, 0:8]),
          r=[ctx.pbt[1]], w=[tok_c2r])
    ctx.t(lambda e: e.matmul(ctx.pb[2][:, 0:8], ctx.ones_row_f[0:1, :], c2row_f[0:1, 0:8],
                             start=True, stop=True),
          r=[ctx.tok_orf, tok_c2r], w=[ctx.pbt[2]])
    ctx.v(lambda e: e.tensor_copy(c2b_f[:, 0:8], ctx.pb[2][:, 0:8]),
          r=[ctx.pbt[2]], w=[tok_c2b])
    return dict(gws_f=gws_f, tok_gws=tok_gws, negc1_f=negc1_f, tok_nc1=tok_nc1,
                c2b_f=c2b_f, tok_c2b=tok_c2b)



def emit_ln_gate(ctx, layer):
    ab, af, d = ctx.ab, ctx.af, ctx.dram
    x_d = d["x_fm"] if layer == 0 else d["X1"]
    a_d = d["A0"] if layer == 0 else d["A1"]
    NE = 8 if layer == 0 else 16
    w_em, tok_wem = (ctx.w_em0, ctx.tok_wem0) if layer == 0 else (ctx.g_em1, ctx.tok_gem1)
    gc = emit_gate_consts(ctx, layer)

    xf = [af.alloc(NCH * TB) for _ in range(2)]
    xnf = [af.alloc(NCH * TB) for _ in range(2)]
    xsq = [af.alloc(TB) for _ in range(2)]
    t1 = af.alloc(TB)
    rows = [af.alloc(2 * TB) for _ in range(2)]
    asb = [ab.alloc(NCH * TB) for _ in range(2)]
    gev = af.alloc(TB)
    lf = [af.alloc(TB) for _ in range(2)]
    gsc = [af.alloc(96) for _ in range(4)]
    wtm = [ab.alloc(16) for _ in range(4)]

    pb, pbt = ctx.pb, ctx.pbt

    for tb in range(NTB):
        t0 = tb * TB
        xb, xt = xf[tb % 2]
        xn, xnt = xnf[tb % 2]
        sqb, sqt = xsq[tb % 2]
        ab_, at = asb[tb % 2]
        rw, rwt = rows[tb % 2]
        lb, lbt = lf[tb % 2]

        ctx.dy(lambda e, xb=xb, t0=t0: e.dma_start(
            out=xb.rearrange("p (c t) -> p c t", c=NCH), in_=fmv(x_d, t0, t0 + TB)),
            w=[xt])
        for c in range(NCH):
            ctx.t(lambda e, xb=xb, c=c: e.matmul(
                pb[0][0:1, :], ctx.ones_col_f[:, 0:1], xb[:, c * TB:(c + 1) * TB],
                start=(c == 0), stop=(c == NCH - 1)), r=[xt, ctx.tok_ocf], w=[pbt[0]])
        for c in range(NCH):
            ctx.s(lambda e, xb=xb, sqb=sqb, c=c: e.activation(
                out=sqb, in_=xb[:, c * TB:(c + 1) * TB], func=AF.Square),
                r=[xt], w=[sqt])
            ctx.t(lambda e, sqb=sqb, c=c: e.matmul(
                pb[1][0:1, :], ctx.ones_col_f[:, 0:1], sqb,
                start=(c == 0), stop=(c == NCH - 1)), r=[sqt, ctx.tok_ocf], w=[pbt[1]])
        ctx.v(lambda e, rw=rw: e.tensor_scalar_mul(rw[0:1, 0:TB], pb[0][0:1, :], 1.0 / D),
              r=[pbt[0]], w=[rwt])
        ctx.v(lambda e, rw=rw: e.tensor_scalar_mul(rw[0:1, TB:2 * TB], pb[1][0:1, :], 1.0 / D),
              r=[pbt[1]], w=[rwt])
        ctx.s(lambda e, rw=rw: e.activation(out=t1[0][0:1, 0:TB], in_=rw[0:1, 0:TB],
                                            func=AF.Square), r=[rwt], w=[t1[1]])
        ctx.v(lambda e, rw=rw: e.tensor_sub(rw[0:1, TB:2 * TB], rw[0:1, TB:2 * TB],
                                            t1[0][0:1, 0:TB]), r=[rwt, t1[1]], w=[rwt])
        ctx.s(lambda e, rw=rw: e.activation(out=rw[0:1, TB:2 * TB], in_=rw[0:1, TB:2 * TB],
                                            func=AF.Ln, bias=1e-5), r=[rwt], w=[rwt])
        ctx.s(lambda e, rw=rw: e.activation(out=rw[0:1, TB:2 * TB], in_=rw[0:1, TB:2 * TB],
                                            func=AF.Exp, scale=-0.5), r=[rwt], w=[rwt])
        ctx.v(lambda e, rw=rw: e.tensor_mul(rw[0:1, 0:TB], rw[0:1, 0:TB], rw[0:1, TB:2 * TB]),
              r=[rwt], w=[rwt])
        ctx.t(lambda e, rw=rw: e.matmul(pb[2][:, :], ctx.ones_row_f[0:1, :],
                                        rw[0:1, TB:2 * TB], start=True, stop=True),
              r=[rwt, ctx.tok_orf], w=[pbt[2]])
        ctx.t(lambda e, rw=rw: e.matmul(pb[3][:, :], ctx.ones_row_f[0:1, :],
                                        rw[0:1, 0:TB], start=True, stop=True),
              r=[rwt, ctx.tok_orf], w=[pbt[3]])
        for c in range(NCH):
            ctx.v(lambda e, xb=xb, c=c: e.tensor_mul(
                t1[0], xb[:, c * TB:(c + 1) * TB], pb[2][:, :]), r=[xt, pbt[2]], w=[t1[1]])
            ctx.v(lambda e, xn=xn, c=c: e.tensor_sub(
                xn[:, c * TB:(c + 1) * TB], t1[0], pb[3][:, :]), r=[t1[1], pbt[3]], w=[xnt])
            ctx.v(lambda e, xn=xn, ab_=ab_, c=c: e.tensor_copy(
                ab_[:, c * TB:(c + 1) * TB], xn[:, c * TB:(c + 1) * TB]), r=[xnt], w=[at])
        ctx.dy(lambda e, ab_=ab_, t0=t0: e.dma_start(
            out=fmv(a_d, t0, t0 + TB), in_=ab_.rearrange("p (c t) -> p c t", c=NCH)),
            r=[at], w=[("d", "A", layer, tb)])

        if layer == 0:
            for c in range(NCH):
                ctx.t(lambda e, xn=xn, c=c: e.matmul(
                    pb[4][0:8, :], gc['gws_f'][:, c * 8:(c + 1) * 8],
                    xn[:, c * TB:(c + 1) * TB], start=(c == 0), stop=False),
                    r=[xnt, gc['tok_gws']], w=[pbt[4]])
            for c in range(NCH):
                ctx.t(lambda e, xn=xn, c=c: e.matmul(
                    pb[0][0:1, :], ctx.ones_col_f[:, 0:1], xn[:, c * TB:(c + 1) * TB],
                    start=(c == 0), stop=(c == NCH - 1)), r=[xnt, ctx.tok_ocf], w=[pbt[0]])
            for c in range(NCH):
                ctx.s(lambda e, xn=xn, sqb=sqb, c=c: e.activation(
                    out=sqb, in_=xn[:, c * TB:(c + 1) * TB], func=AF.Square),
                    r=[xnt], w=[sqt])
                ctx.t(lambda e, sqb=sqb, c=c: e.matmul(
                    pb[1][0:1, :], ctx.ones_col_f[:, 0:1], sqb,
                    start=(c == 0), stop=(c == NCH - 1)), r=[sqt, ctx.tok_ocf], w=[pbt[1]])
            ctx.v(lambda e, rw=rw: e.tensor_scalar_mul(rw[0:1, 0:TB], pb[0][0:1, :], 1.0 / D),
                  r=[pbt[0]], w=[rwt])
            ctx.v(lambda e, rw=rw: e.tensor_scalar_mul(rw[0:1, TB:2 * TB], pb[1][0:1, :], 1.0 / D),
                  r=[pbt[1]], w=[rwt])
            ctx.s(lambda e, rw=rw: e.activation(out=t1[0][0:1, 0:TB], in_=rw[0:1, 0:TB],
                                                func=AF.Square), r=[rwt], w=[t1[1]])
            ctx.v(lambda e, rw=rw: e.tensor_sub(rw[0:1, TB:2 * TB], rw[0:1, TB:2 * TB],
                                                t1[0][0:1, 0:TB]), r=[rwt, t1[1]], w=[rwt])
            ctx.s(lambda e, rw=rw: e.activation(out=rw[0:1, TB:2 * TB], in_=rw[0:1, TB:2 * TB],
                                                func=AF.Ln, bias=1e-5), r=[rwt], w=[rwt])
            ctx.s(lambda e, rw=rw: e.activation(out=rw[0:1, TB:2 * TB], in_=rw[0:1, TB:2 * TB],
                                                func=AF.Exp, scale=-0.5), r=[rwt], w=[rwt])
            ctx.t(lambda e, rw=rw: e.matmul(pb[4][0:8, :], gc['negc1_f'][0:1, 0:8],
                                            rw[0:1, 0:TB], start=False, stop=True),
                  r=[rwt, gc['tok_nc1']], w=[pbt[4]])
            ctx.t(lambda e, rw=rw: e.matmul(pb[5][0:8, :], ctx.ones_row_f[0:1, 0:8],
                                            rw[0:1, TB:2 * TB], start=True, stop=True),
                  r=[rwt, ctx.tok_orf], w=[pbt[5]])
            ctx.s(lambda e: e.activation(out=gev[0][0:8, 0:TB], in_=pb[4][0:8, :],
                                         func=AF.Copy), r=[pbt[4]], w=[gev[1]])
            ctx.v(lambda e, lb=lb: e.tensor_mul(lb[0:8, 0:TB], gev[0][0:8, 0:TB],
                                                pb[5][0:8, :]), r=[gev[1], pbt[5]], w=[lbt])
        else:
            for c in range(NCH):
                ctx.t(lambda e, xn=xn, c=c: e.matmul(
                    pb[4][0:16, :], gc['cent_f'][:, c * 16:(c + 1) * 16],
                    xn[:, c * TB:(c + 1) * TB], start=(c == 0), stop=(c == NCH - 1)),
                    r=[xnt, gc['tok_cent']], w=[pbt[4]])
            ctx.v(lambda e, lb=lb: e.tensor_scalar(
                lb[0:16, 0:TB], pb[4][0:16, :], gc['eb_f'][0:16, 0:1], None, ALU.add),
                r=[pbt[4], gc['tok_eb']], w=[lbt])

        if "GLOG" in ctx.debug:
            if "GLOG" not in ctx.dram:
                ctx.dt("GLOG", (16, S), F32, "ExternalOutput")
            ctx.dy(lambda e, lb=lb, t0=t0: e.dma_start(
                out=ctx.dram["GLOG"][0:NE, t0:t0 + TB], in_=lb[0:NE, 0:TB]), r=[lbt])

        for j in range(4):
            gs, gst = gsc[j]
            wt, wtt = wtm[j]
            pcol = j * NE
            ctx.t(lambda e, lb=lb, j=j, pcol=pcol: e.transpose(
                pb[5][:, 256 + pcol:256 + pcol + NE], lb[0:NE, j * 128:(j + 1) * 128],
                ctx.ident_f[0:NE, 0:NE]),
                r=[lbt, ctx.tok_idf], w=[pbt[5]])
            ltm = gs[:, 0:NE]
            e_ = gs[:, 16:16 + NE]
            w_ = gs[:, 32:32 + NE]
            z1 = gs[:, 48:48 + NE]
            z2 = gs[:, 64:64 + NE]
            nm = gs[:, 80:81]
            S_ = gs[:, 81:82]
            m1 = gs[:, 82:83]
            m2 = gs[:, 83:84]
            dn = gs[:, 84:85]
            if layer == 0:
                ctx.v(lambda e, pcol=pcol, ltm=ltm: e.tensor_add(
                    ltm, pb[5][:, 256 + pcol:256 + pcol + NE], gc['c2b_f'][:, 0:8]),
                    r=[pbt[5], gc['tok_c2b']], w=[gst])
            else:
                ctx.v(lambda e, pcol=pcol, ltm=ltm: e.tensor_copy(
                    ltm, pb[5][:, 256 + pcol:256 + pcol + NE]), r=[pbt[5]], w=[gst])
            ctx.v(lambda e, ltm=ltm, nm=nm: e.tensor_reduce(
                nm, ltm, AX.X, ALU.max, negate=True), r=[gst], w=[gst])
            ctx.s(lambda e, ltm=ltm, e_=e_, nm=nm, S_=S_: e.activation(
                out=e_, in_=ltm, func=AF.Exp, bias=nm, accum_out=S_), r=[gst], w=[gst])
            ctx.v(lambda e, e_=e_, m1=m1: e.tensor_reduce(m1, e_, AX.X, ALU.max),
                  r=[gst], w=[gst])
            ctx.v(lambda e, e_=e_, z1=z1, m1=m1: e.match_replace(
                z1, m1.broadcast_to((128, 8)), e_, 0.0), r=[gst], w=[gst])
            ctx.v(lambda e, z1=z1, m2=m2: e.tensor_reduce(m2, z1, AX.X, ALU.max),
                  r=[gst], w=[gst])
            ctx.v(lambda e, z1=z1, z2=z2, m2=m2: e.match_replace(
                z2, m2.broadcast_to((128, 8)), z1, 0.0), r=[gst], w=[gst])
            if layer == 0:
                ctx.v(lambda e, m1=m1, m2=m2, dn=dn: e.tensor_add(dn, m1, m2),
                      r=[gst], w=[gst])
                ctx.v(lambda e, S_=S_, dn=dn: e.scalar_tensor_tensor(
                    dn, S_, 1e-5, dn, ALU.mult, ALU.add), r=[gst], w=[gst])
                ctx.v(lambda e, dn=dn: e.reciprocal(dn, dn), r=[gst], w=[gst])
            else:
                ctx.v(lambda e, S_=S_, dn=dn: e.reciprocal(dn, S_), r=[gst], w=[gst])
            ctx.v(lambda e, e_=e_, z2=z2, w_=w_: e.tensor_sub(w_, e_, z2),
                  r=[gst], w=[gst])
            ctx.v(lambda e, w_=w_, wt=wt, dn=dn: e.tensor_scalar(
                wt[:, 0:NE], w_, dn, None, ALU.mult), r=[gst], w=[wtt])
            if "GTOP" in ctx.debug and tb == 0 and j == 0:
                if "GTOP" not in ctx.dram:
                    ctx.dt("GTOP", (128, 96), F32, "ExternalOutput")
                ctx.dy(lambda e, gs=gs: e.dma_start(
                    out=ctx.dram["GTOP"], in_=gs[:, 0:96]), r=[gst])
            ctx.t(lambda e, wt=wt, j=j: e.transpose(
                ctx.pt[1][0:NE, j * 128:(j + 1) * 128], wt[:, 0:NE], ctx.ident_b),
                r=[wtt, ctx.tok_idb], w=[ctx.ptt[1]])
            ctx.v(lambda e, t0=t0, j=j: e.tensor_copy(
                w_em[0:NE, t0 + j * 128:t0 + (j + 1) * 128],
                ctx.pt[1][0:NE, j * 128:(j + 1) * 128]), r=[ctx.ptt[1]], w=[tok_wem])
    ctx.phase()



def emit_proj(ctx, layer):
    ab, af, d = ctx.ab, ctx.af, ctx.dram
    a_d = d["A0"] if layer == 0 else d["A1"]
    pb, pbt, pt, ptt = ctx.pb, ctx.pbt, ctx.pt, ctx.ptt

    if layer == 0:
        passes = [("q", "w0_Wq", "w0_Wq_sw", "Q0"), ("k", "w0_Wk", "w0_Wk_sw", "K0"),
                  ("v", "w0_Wv", None, "V0")]
        cos_f, tok_cos = af.alloc(S)
        sin_f, tok_sin = af.alloc(S)
        ctx.dy(lambda e: e.dma_start(out=cos_f, in_=d["c_cos"]), w=[tok_cos])
        ctx.dy(lambda e: e.dma_start(out=sin_f, in_=d["c_sin"]), w=[tok_sin])
    else:
        passes = [("q", "w1_wq", None, "Q1"), ("k", "w1_wk", None, "K1"),
                  ("v", "w1_wv", None, "V1")]
        gmrow_f, tok_gmr = af.alloc(2048)
        ctx.das(lambda e: e.dma_start(out=gmrow_f[0:1, 0:1024], in_=d["w1_qg"]), w=[tok_gmr])
        ctx.das(lambda e: e.dma_start(out=gmrow_f[0:1, 1024:2048], in_=d["w1_kg"]), w=[tok_gmr])

    wbuf = [ab.alloc(NCH * D) for _ in range(2)]
    atile = [ab.alloc(NCH * TB) for _ in range(2)]
    otile = [ab.alloc(NCH * TB) for _ in range(2)]
    vasm = [ab.alloc(4 * D) for _ in range(2)]
    tA = [af.alloc(TB) for _ in range(2)]
    tB = [af.alloc(TB) for _ in range(2)]
    rr = [af.alloc(TB) for _ in range(2)]

    def load_w(nm, slot):
        wb, wt = wbuf[slot]
        ctx.dg(lambda e, wb=wb, nm=nm: e.dma_start(
            out=wb.rearrange("p (c o) -> p c o", c=NCH),
            in_=d[nm].rearrange("(c p) o -> p c o", p=128)), w=[wt])
        return wb, wt

    def one_pass(kind, wnm, swnm, out_nm):
        out_d = d[out_nm]
        wb, wt = load_w(wnm, 0)
        wsb = wst = None
        if swnm:
            wsb, wst = load_w(swnm, 1)
        rope = (layer == 0 and kind in ("q", "k"))
        rms = (layer == 1 and kind in ("q", "k"))
        gcol = 0 if kind == "q" else 1024
        for tb in range(NTB):
            t0 = tb * TB
            ati, att = atile[tb % 2]
            oti, ott = otile[tb % 2]
            va, vat = vasm[tb % 2]
            ta, tat = tA[tb % 2]
            tb_, tbt = tB[tb % 2]
            rw, rwt = rr[tb % 2]
            ctx.dy(lambda e, ati=ati, t0=t0: e.dma_start(
                out=ati.rearrange("p (c t) -> p c t", c=NCH), in_=fmv(a_d, t0, t0 + TB)),
                w=[att])
            for o in range(NCH):
                bank = o % 2
                for c in range(NCH):
                    ctx.t(lambda e, wb=wb, ati=ati, c=c, o=o, bank=bank: e.matmul(
                        pb[bank][:, :], wb[:, c * D + o * 128:c * D + (o + 1) * 128],
                        ati[:, c * TB:(c + 1) * TB],
                        start=(c == 0), stop=(c == NCH - 1)),
                        r=[wt, att], w=[pbt[bank]])
                if rope:
                    for c in range(NCH):
                        ctx.t(lambda e, wsb=wsb, ati=ati, c=c, o=o, bank=bank: e.matmul(
                            pb[2 + bank][:, :], wsb[:, c * D + o * 128:c * D + (o + 1) * 128],
                            ati[:, c * TB:(c + 1) * TB],
                            start=(c == 0), stop=(c == NCH - 1)),
                            r=[wst, att], w=[pbt[2 + bank]])
                    ctx.v(lambda e, ta=ta, t0=t0, bank=bank: e.tensor_mul(
                        ta, pb[bank][:, :], cos_f[:, t0:t0 + TB]),
                        r=[pbt[bank], tok_cos], w=[tat])
                    ctx.v(lambda e, tb_=tb_, t0=t0, bank=bank: e.tensor_mul(
                        tb_, pb[2 + bank][:, :], sin_f[:, t0:t0 + TB]),
                        r=[pbt[2 + bank], tok_sin], w=[tbt])
                    ctx.g(lambda e, oti=oti, ta=ta, tb_=tb_, o=o: e.tensor_add(
                        oti[:, o * TB:(o + 1) * TB], ta, tb_), r=[tat, tbt], w=[ott])
                elif rms:
                    ctx.s(lambda e, oti=oti, o=o, bank=bank: e.activation(
                        out=oti[:, o * TB:(o + 1) * TB], in_=pb[bank][:, :], func=AF.Copy),
                        r=[pbt[bank]], w=[ott])
                    ctx.s(lambda e, ta=ta, bank=bank: e.activation(
                        out=ta, in_=pb[bank][:, :], func=AF.Square),
                        r=[pbt[bank]], w=[tat])
                    ctx.t(lambda e, ta=ta, o=o: e.matmul(
                        pb[4][0:1, :], ctx.ones_col_f[:, 0:1], ta,
                        start=(o == 0), stop=(o == NCH - 1)), r=[tat, ctx.tok_ocf], w=[pbt[4]])
                else:
                    ctx.s(lambda e, oti=oti, o=o, bank=bank: e.activation(
                        out=oti[:, o * TB:(o + 1) * TB], in_=pb[bank][:, :], func=AF.Copy),
                        r=[pbt[bank]], w=[ott])
                    for ts in range(4):
                        ctx.t(lambda e, oti=oti, o=o, ts=ts: e.transpose(
                            pt[0][:, (ts % 2) * 128 + 256:(ts % 2) * 128 + 384],
                            oti[:, o * TB + ts * 128:o * TB + (ts + 1) * 128],
                            ctx.ident_b), r=[ott, ctx.tok_idb], w=[ptt[0]])
                        ctx.v(lambda e, va=va, o=o, ts=ts: e.tensor_copy(
                            va[:, ts * D + o * 128:ts * D + (o + 1) * 128],
                            pt[0][:, (ts % 2) * 128 + 256:(ts % 2) * 128 + 384]),
                            r=[ptt[0]], w=[vat])
            if rms:
                gm = gmrow_f
                ctx.v(lambda e, rw=rw: e.tensor_scalar_mul(
                    rw[0:1, 0:TB], pb[4][0:1, :], 1.0 / D), r=[pbt[4]], w=[rwt])
                ctx.s(lambda e, rw=rw: e.activation(out=rw[0:1, 0:TB], in_=rw[0:1, 0:TB],
                                                    func=AF.Ln, bias=1e-6), r=[rwt], w=[rwt])
                ctx.s(lambda e, rw=rw: e.activation(out=rw[0:1, 0:TB], in_=rw[0:1, 0:TB],
                                                    func=AF.Exp, scale=-0.5), r=[rwt], w=[rwt])
                for o in range(NCH):
                    ctx.t(lambda e, rw=rw, o=o: e.matmul(
                        pb[5][:, :], gm[0:1, gcol + o * 128:gcol + (o + 1) * 128],
                        rw[0:1, 0:TB], start=True, stop=True),
                        r=[rwt, tok_gmr], w=[pbt[5]])
                    ctx.v(lambda e, oti=oti, o=o: e.tensor_mul(
                        oti[:, o * TB:(o + 1) * TB], oti[:, o * TB:(o + 1) * TB],
                        pb[5][:, :]), r=[ott, pbt[5]], w=[ott])
            if kind == "v":
                ctx.dy(lambda e, va=va, t0=t0: e.dma_start(
                    out=out_d[t0:t0 + TB, :].rearrange("(ts p) d -> p ts d", p=128),
                    in_=va.rearrange("p (ts d) -> p ts d", ts=4)),
                    r=[vat], w=[("d", out_nm, tb)])
            else:
                ctx.dy(lambda e, oti=oti, t0=t0: e.dma_start(
                    out=fmv(out_d, t0, t0 + TB),
                    in_=oti.rearrange("p (c t) -> p c t", c=NCH)),
                    r=[ott], w=[("d", out_nm, tb)])

    for kind, wnm, swnm, out_nm in passes:
        one_pass(kind, wnm, swnm, out_nm)
    ctx.phase()



def emit_attn(ctx, layer):
    ab, af, d = ctx.ab, ctx.af, ctx.dram
    pb, pbt, pt, ptt = ctx.pb, ctx.pbt, ctx.pt, ctx.ptt
    NQT = S // 128
    if layer == 0:
        q_d, k_d, v_d, s_d, wo_nm = d["Q0"], d["K0"], d["V0"], d["S0"], "w0_Wo"
    else:
        q_d, k_d, v_d, s_d, wo_nm = d["Q1"], d["K1"], d["V1"], d["S1"], "w1_wo"

    wo_sb, tok_wo = ab.alloc(NCH * D)
    ctx.dg(lambda e: e.dma_start(
        out=wo_sb.rearrange("p (c o) -> p c o", c=NCH),
        in_=d[wo_nm].rearrange("(c p) o -> p c o", p=128)), w=[tok_wo])

    if layer == 0:
        k_sb, tok_k = ab.alloc(NCH * S)
        v_sb, tok_v = ab.alloc(NQT * D)
        for half in range(2):
            ctx.dy(lambda e, half=half: e.dma_start(
                out=k_sb.rearrange("p (c t) -> p c t", t=S)[:, :, half * 2048:(half + 1) * 2048],
                in_=k_d[:, half * 2048:(half + 1) * 2048].rearrange("(c p) t -> p c t", p=128)),
                w=[tok_k])
            ctx.dy(lambda e, half=half: e.dma_start(
                out=v_sb.rearrange("p (ts dd) -> p ts dd", dd=D)[:, half * 16:(half + 1) * 16, :],
                in_=v_d[half * 2048:(half + 1) * 2048, :].rearrange("(ts p) dd -> p ts dd", p=128)),
                w=[tok_v])

        def k_ap(h, c, nc_):
            return k_sb[(h % 2) * 64:(h % 2) * 64 + 64,
                        (h // 2) * S + c * 128:(h // 2) * S + (c + nc_) * 128]

        def v_ap(h, c):
            return v_sb[:, c * D + h * 64:c * D + h * 64 + 64]

        def tok_kc(c):
            return tok_k

        def tok_vc(c):
            return tok_v
    else:
        NSL = 6
        k_sb, _tk = ab.alloc(NCH * NSL * 128)
        v_sb, _tv = ab.alloc(NSL * D)
        ktoks = [("kw", i) for i in range(NSL)]
        vtoks = [("vw", i) for i in range(NSL)]

        def load_chunk(c):
            slot = c % NSL
            ctx.dy(lambda e, c=c, slot=slot: e.dma_start(
                out=k_sb.rearrange("p (ch x) -> p ch x", x=NSL * 128)[:, :, slot * 128:(slot + 1) * 128],
                in_=k_d[:, c * 128:(c + 1) * 128].rearrange("(ch p) t -> p ch t", p=128)),
                w=[ktoks[slot]])
            ctx.dy(lambda e, c=c, slot=slot: e.dma_start(
                out=v_sb[:, slot * D:(slot + 1) * D],
                in_=v_d[c * 128:(c + 1) * 128, :]), w=[vtoks[slot]])

        def k_ap(h, c, nc_):
            assert nc_ == 1
            slot = c % NSL
            base = (h // 2) * (NSL * 128)
            return k_sb[(h % 2) * 64:(h % 2) * 64 + 64, base + slot * 128:base + (slot + 1) * 128]

        def v_ap(h, c):
            slot = c % NSL
            return v_sb[:, slot * D + h * 64:slot * D + h * 64 + 64]

        def tok_kc(c):
            return ktoks[c % NSL]

        def tok_vc(c):
            return vtoks[c % NSL]

        dsl_f, tok_dsl = af.alloc(WIN)
        msl_f, tok_msl = af.alloc(WIN)
        ddg_f, tok_ddg = af.alloc(128)
        mdg_f, tok_mdg = af.alloc(128)
        ctx.dy(lambda e: e.dma_start(out=dsl_f, in_=d["c_dslab"]), w=[tok_dsl])
        ctx.dy(lambda e: e.dma_start(out=msl_f, in_=d["c_mslab"]), w=[tok_msl])
        ctx.dy(lambda e: e.dma_start(out=ddg_f, in_=d["c_ddiag"]), w=[tok_ddg])
        ctx.dy(lambda e: e.dma_start(out=mdg_f, in_=d["c_mdiag"]), w=[tok_mdg])
        bs_f, tok_bs = af.alloc(NH * WIN)
        bd_f, tok_bd = af.alloc(NH * 128)
        for h in range(NH):
            sl = 2.0 ** (-8.0 / NH * (h + 1))
            ctx.v(lambda e, h=h, sl=sl: e.scalar_tensor_tensor(
                bs_f[:, h * WIN:(h + 1) * WIN], dsl_f, sl, msl_f, ALU.mult, ALU.add),
                r=[tok_dsl, tok_msl], w=[tok_bs])
            ctx.v(lambda e, h=h, sl=sl: e.scalar_tensor_tensor(
                bd_f[:, h * 128:(h + 1) * 128], ddg_f, sl, mdg_f, ALU.mult, ALU.add),
                r=[tok_ddg, tok_mdg], w=[tok_bd])

    GQT = 2
    psz = TB if layer == 0 else TB + 128
    qt_b = [ab.alloc(NCH * 128) for _ in range(2)]
    o_sb, tok_o = af.alloc(GQT * D)
    p_sb = [ab.alloc(psz) for _ in range(4)]
    ptv = [ab.alloc(psz) for _ in range(4)]
    fl_a = af.alloc(32)
    if layer == 1:
        ssb = [af.alloc(TB + 128) for _ in range(2)]
    ofm_g, tok_ofm = ab.alloc(NCH * GQT * 128)
    s_st = [af.alloc(GQT * 128) for _ in range(2)]

    def l0_head(qt, h, qb, qbt):
        nchunks = qt + 1
        slabs = []
        c0 = 0
        while c0 < nchunks:
            cn = min(4, nchunks - c0)
            if nchunks - (c0 + cn) == 1:
                cn -= 1
            slabs.append((c0, cn))
            c0 += cn
        nsl = len(slabs)
        qap = qb[(h % 2) * 64:(h % 2) * 64 + 64, (h // 2) * 128:(h // 2) * 128 + 128]
        fl, flt = fl_a[0], ("fl", h % 2)
        fo = 16 * (h % 2)
        MN, LS, BI, CR, RS, NM = fo + 0, fo + 1, fo + 2, fo + 3, fo + 4, fo + 5
        obank = 4 + (h % 2)
        qtl = qt % GQT
        ofrag = o_sb[:, qtl * D + h * 64:qtl * D + (h + 1) * 64]
        for si, (c0, cn) in enumerate(slabs):
            first, last = si == 0, si == nsl - 1
            diag = (c0 + cn == nchunks)
            nk = cn * 128
            bank = 2 * (si % 2) + (h % 2)
            pbuf, pbt_ = p_sb[2 * (h % 2) + si % 2]
            ptb, ptt_ = ptv[2 * (h % 2) + si % 2]
            ctx.t(lambda e, qap=qap, h=h, c0=c0, cn=cn, nk=nk, bank=bank, diag=diag: e.matmul(
                pb[bank][:, 0:nk], qap, k_ap(h, c0, cn), start=True, stop=not diag,
                tile_position=((h % 2) * 64, 0)),
                r=[qbt] + [tok_kc(c0 + i) for i in range(cn)], w=[pbt[bank]])
            if diag:
                ctx.t(lambda e, nk=nk, bank=bank: e.matmul(
                    pb[bank][:, nk - 128:nk], ctx.mask0T_b, ctx.ident_b,
                    start=False, stop=True),
                    r=[ctx.tok_m0, ctx.tok_idb], w=[pbt[bank]])
            ctx.v(lambda e, nk=nk, bank=bank, NM=NM: e.tensor_reduce(
                fl[:, NM:NM + 1], pb[bank][:, 0:nk], AX.X, ALU.max, negate=True),
                r=[pbt[bank]], w=[flt])
            if first:
                ctx.g(lambda e, NM=NM, MN=MN: e.tensor_copy(
                    fl[:, MN:MN + 1], fl[:, NM:NM + 1]), r=[flt], w=[flt])
            else:
                ctx.v(lambda e, NM=NM, MN=MN, CR=CR: e.tensor_tensor(
                    fl[:, CR:CR + 1], fl[:, MN:MN + 1], fl[:, NM:NM + 1], ALU.min),
                    r=[flt], w=[flt])
                ctx.g(lambda e, MN=MN, CR=CR: e.tensor_sub(
                    fl[:, MN:MN + 1], fl[:, CR:CR + 1], fl[:, MN:MN + 1]),
                    r=[flt], w=[flt])
                ctx.s(lambda e, MN=MN, CR=CR: e.activation(
                    out=fl[:, CR + 1:CR + 2], in_=fl[:, MN:MN + 1], func=AF.Exp,
                    scale=0.125), r=[flt], w=[flt])
                ctx.g(lambda e, MN=MN, CR=CR: e.tensor_copy(
                    fl[:, MN:MN + 1], fl[:, CR:CR + 1]), r=[flt], w=[flt])
            ctx.g(lambda e, MN=MN, BI=BI: e.tensor_scalar_mul(
                fl[:, BI:BI + 1], fl[:, MN:MN + 1], 0.125), r=[flt], w=[flt])
            ctx.s(lambda e, pbuf=pbuf, nk=nk, bank=bank, BI=BI, NM=NM: e.activation(
                out=pbuf[:, 0:nk], in_=pb[bank][:, 0:nk], func=AF.Exp,
                scale=0.125, bias=fl[:, BI:BI + 1], accum_out=fl[:, NM:NM + 1]),
                r=[pbt[bank], flt], w=[pbt_, flt])
            if first:
                ctx.g(lambda e, LS=LS, NM=NM: e.tensor_copy(
                    fl[:, LS:LS + 1], fl[:, NM:NM + 1]), r=[flt], w=[flt])
            else:
                ctx.g(lambda e, LS=LS, RS=RS: e.tensor_mul(
                    fl[:, LS:LS + 1], fl[:, LS:LS + 1], fl[:, RS:RS + 1]),
                    r=[flt], w=[flt])
                ctx.g(lambda e, LS=LS, NM=NM: e.tensor_add(
                    fl[:, LS:LS + 1], fl[:, LS:LS + 1], fl[:, NM:NM + 1]),
                    r=[flt], w=[flt])
            tb_bank = h % 2
            for kc in range(cn):
                ctx.t(lambda e, pbuf=pbuf, kc=kc, tb_bank=tb_bank: e.transpose(
                    pt[tb_bank][:, kc * 128:(kc + 1) * 128], pbuf[:, kc * 128:(kc + 1) * 128],
                    ctx.ident_b), r=[pbt_, ctx.tok_idb], w=[ptt[tb_bank]])
            if (h + si) % 2 == 0:
                ctx.v(lambda e, ptb=ptb, nk=nk, tb_bank=tb_bank: e.tensor_copy(
                    ptb[:, 0:nk], pt[tb_bank][:, 0:nk]), r=[ptt[tb_bank]], w=[ptt_])
            else:
                ctx.s(lambda e, ptb=ptb, nk=nk, tb_bank=tb_bank: e.activation(
                    out=ptb[:, 0:nk], in_=pt[tb_bank][:, 0:nk], func=AF.Copy),
                    r=[ptt[tb_bank]], w=[ptt_])
            for kc in range(cn):
                ctx.t(lambda e, ptb=ptb, h=h, c0=c0, kc=kc, cn=cn: e.matmul(
                    pb[4 + (h % 2)][:, 0:64], ptb[:, kc * 128:(kc + 1) * 128],
                    v_ap(h, c0 + kc), start=(kc == 0), stop=(kc == cn - 1)),
                    r=[ptt_, tok_vc(c0 + kc)], w=[pbt[4 + (h % 2)]])
            if first:
                ctx.v(lambda e, ofrag=ofrag, h=h: e.tensor_copy(
                    ofrag, pb[4 + (h % 2)][:, 0:64]), r=[pbt[4 + (h % 2)]], w=[tok_o])
            else:
                ctx.v(lambda e, ofrag=ofrag, h=h, RS=RS: e.scalar_tensor_tensor(
                    ofrag, ofrag, fl[:, RS:RS + 1], pb[4 + (h % 2)][:, 0:64],
                    ALU.mult, ALU.add), r=[pbt[4 + (h % 2)], flt, tok_o], w=[tok_o])
        ctx.v(lambda e, LS=LS, RS=RS: e.reciprocal(fl[:, RS:RS + 1], fl[:, LS:LS + 1]),
              r=[flt], w=[flt])
        ctx.v(lambda e, ofrag=ofrag, RS=RS: e.tensor_scalar(
            ofrag, ofrag, fl[:, RS:RS + 1], None, ALU.mult), r=[flt, tok_o], w=[tok_o])

    def l1_head(qt, h, qb, qbt):
        lo = max(0, qt - 4)
        ncw = qt - lo
        nk = ncw * 128
        nall = nk + 128
        qap = qb[(h % 2) * 64:(h % 2) * 64 + 64, (h // 2) * 128:(h // 2) * 128 + 128]
        fl, flt = fl_a[0], ("fl", h % 2)
        fo = 16 * (h % 2)
        LS, RS, NM = fo + 1, fo + 4, fo + 5
        bank = h % 2
        sb_, sbt_ = ssb[h % 2]
        pbuf, pbt_ = p_sb[h % 2]
        ptb, ptt_ = ptv[h % 2]
        qtl = qt % GQT
        ofrag = o_sb[:, qtl * D + h * 64:qtl * D + (h + 1) * 64]
        for i in range(ncw):
            ctx.t(lambda e, qap=qap, h=h, i=i, lo=lo, bank=bank: e.matmul(
                pb[bank][:, i * 128:(i + 1) * 128], qap, k_ap(h, lo + i, 1),
                start=True, stop=True, tile_position=((h % 2) * 64, 0)),
                r=[qbt, tok_kc(lo + i)], w=[pbt[bank]])
        ctx.t(lambda e, qap=qap, h=h, qt=qt, bank=bank: e.matmul(
            pb[2 + bank][:, 0:128], qap, k_ap(h, qt, 1),
            start=True, stop=True, tile_position=((h % 2) * 64, 0)),
            r=[qbt, tok_kc(qt)], w=[pbt[2 + bank]])
        if ncw > 0:
            ctx.v(lambda e, sb_=sb_, h=h, nk=nk, bank=bank: e.scalar_tensor_tensor(
                sb_[:, 0:nk], pb[bank][:, 0:nk], 0.125,
                bs_f[:, (h + 1) * WIN - nk:(h + 1) * WIN], ALU.mult, ALU.add),
                r=[pbt[bank], tok_bs], w=[sbt_])
        ctx.v(lambda e, sb_=sb_, h=h, nk=nk, bank=bank: e.scalar_tensor_tensor(
            sb_[:, nk:nk + 128], pb[2 + bank][:, 0:128], 0.125,
            bd_f[:, h * 128:(h + 1) * 128], ALU.mult, ALU.add),
            r=[pbt[2 + bank], tok_bd], w=[sbt_])
        ctx.v(lambda e, sb_=sb_, nall=nall, NM=NM: e.tensor_reduce(
            fl[:, NM:NM + 1], sb_[:, 0:nall], AX.X, ALU.max, negate=True),
            r=[sbt_], w=[flt])
        ctx.s(lambda e, pbuf=pbuf, sb_=sb_, nall=nall, NM=NM, LS=LS: e.activation(
            out=pbuf[:, 0:nall], in_=sb_[:, 0:nall], func=AF.Exp,
            bias=fl[:, NM:NM + 1], accum_out=fl[:, LS:LS + 1]),
            r=[sbt_, flt], w=[pbt_, flt])
        tb_bank = h % 2
        for kc in range(ncw + 1):
            ctx.t(lambda e, pbuf=pbuf, kc=kc, tb_bank=tb_bank: e.transpose(
                pt[tb_bank][:, kc * 128:(kc + 1) * 128], pbuf[:, kc * 128:(kc + 1) * 128],
                ctx.ident_b), r=[pbt_, ctx.tok_idb], w=[ptt[tb_bank]])
        ctx.v(lambda e, ptb=ptb, nall=nall, tb_bank=tb_bank: e.tensor_copy(
            ptb[:, 0:nall], pt[tb_bank][:, 0:nall]), r=[ptt[tb_bank]], w=[ptt_])
        for kc in range(ncw + 1):
            ctx.t(lambda e, ptb=ptb, h=h, lo=lo, kc=kc, ncw=ncw: e.matmul(
                pb[4 + (h % 2)][:, 0:64], ptb[:, kc * 128:(kc + 1) * 128],
                v_ap(h, lo + kc), start=(kc == 0), stop=(kc == ncw)),
                r=[ptt_, tok_vc(lo + kc)], w=[pbt[4 + (h % 2)]])
        ctx.v(lambda e, LS=LS, RS=RS: e.reciprocal(fl[:, RS:RS + 1], fl[:, LS:LS + 1]),
              r=[flt], w=[flt])
        ctx.v(lambda e, ofrag=ofrag, h=h, RS=RS: e.tensor_scalar(
            ofrag, pb[4 + (h % 2)][:, 0:64], fl[:, RS:RS + 1], None, ALU.mult),
            r=[pbt[4 + (h % 2)], flt], w=[tok_o])

    for grp in range(NQT // GQT):
        for qtl in range(GQT):
            qt = grp * GQT + qtl
            q0 = qt * 128
            qb, qbt = qt_b[qt % 2]
            if layer == 1:
                load_chunk(qt)
            ctx.dy(lambda e, qb=qb, q0=q0: e.dma_start(
                out=qb.rearrange("p (c t) -> p c t", c=NCH), in_=fmv(q_d, q0, q0 + 128)),
                w=[qbt])
            for h in range(NH):
                if layer == 0:
                    l0_head(qt, h, qb, qbt)
                else:
                    l1_head(qt, h, qb, qbt)
        NG = GQT * 128
        g0 = grp * NG
        for qtl in range(GQT):
            for c in range(NCH):
                bk = (qtl * NCH + c) % 2
                ctx.t(lambda e, qtl=qtl, c=c, bk=bk: e.transpose(
                    pb[bk][:, 0:128], o_sb[:, qtl * D + c * 128:qtl * D + (c + 1) * 128],
                    ctx.ident_f), r=[tok_o, ctx.tok_idf], w=[pbt[bk]])
                ctx.v(lambda e, qtl=qtl, c=c, bk=bk: e.tensor_copy(
                    ofm_g[:, c * NG + qtl * 128:c * NG + (qtl + 1) * 128], pb[bk][:, 0:128]),
                    r=[pbt[bk]], w=[tok_ofm])
        for o in range(NCH):
            bk = 2 + o % 2
            for c in range(NCH):
                ctx.t(lambda e, o=o, c=c, bk=bk: e.matmul(
                    pb[bk][:, 0:NG], wo_sb[:, c * D + o * 128:c * D + (o + 1) * 128],
                    ofm_g[:, c * NG:(c + 1) * NG], start=(c == 0), stop=(c == NCH - 1)),
                    r=[tok_wo, tok_ofm], w=[pbt[bk]])
            st_, stt_ = s_st[o % 2]
            ctx.v(lambda e, st_=st_, bk=bk: e.tensor_copy(st_[:, 0:NG], pb[bk][:, 0:NG]),
                  r=[pbt[bk]], w=[stt_])
            ctx.dy(lambda e, st_=st_, o=o, g0=g0: e.dma_start(
                out=s_d[o * 128:(o + 1) * 128, g0:g0 + NG], in_=st_[:, 0:NG]),
                r=[stt_], w=[("d", "s", layer, grp, o)])
    ctx.phase()



def emit_ffn0(ctx):
    ab, af, d = ctx.ab, ctx.af, ctx.dram
    pb, pbt = ctx.pb, ctx.pbt
    QT = 1024
    NTBQ = QT // TB

    e8_sb, tok_e8 = ab.alloc(8 * 128)
    ctx.das(lambda e: e.dma_start(out=e8_sb[0:8, :], in_=d["c_E8"]), w=[tok_e8])

    wg_sb, tok_wg = ab.alloc(NCH * 2048)
    wu_sb, tok_wu = ab.alloc(NCH * 2048)
    wd_sb, tok_wd = ab.alloc(16 * D)
    aq_sb, tok_aq = ab.alloc(NCH * QT)
    out_sb, tok_out = af.alloc(NCH * QT)
    p_sb, tok_p = ab.alloc(16 * TB)
    gs_sb, tok_gs = ab.alloc(TB)
    us_sb, tok_us = ab.alloc(TB)
    wb_sb, tok_wb = ab.alloc(NTBQ * TB)
    sq_f = [af.alloc(TB) for _ in range(2)]
    xq_f = [af.alloc(TB) for _ in range(2)]
    gt_f = [af.alloc(TB) for _ in range(2)]
    x1_f = [af.alloc(TB) for _ in range(2)]

    NEXP = 9

    def wnames(e):
        if e == 8:
            return "f0_sWg", "f0_sWu", "f0_sWd", None
        return "f0_Wg", "f0_Wu", "f0_Wd", e

    def load_we(e):
        g_nm, u_nm, d_nm, idx = wnames(e)
        gd = d[g_nm] if idx is None else d[g_nm][idx]
        ud = d[u_nm] if idx is None else d[u_nm][idx]
        dd = d[d_nm] if idx is None else d[d_nm][idx]
        ctx.dg(lambda e_: e_.dma_start(
            out=wg_sb.rearrange("p (c o) -> p c o", c=NCH),
            in_=gd.rearrange("(c p) o -> p c o", p=128)), w=[tok_wg])
        ctx.dg(lambda e_: e_.dma_start(
            out=wu_sb.rearrange("p (c o) -> p c o", c=NCH),
            in_=ud.rearrange("(c p) o -> p c o", p=128)), w=[tok_wu])
        ctx.dg(lambda e_: e_.dma_start(
            out=wd_sb.rearrange("p (c o) -> p c o", c=16),
            in_=dd.rearrange("(c p) o -> p c o", p=128)), w=[tok_wd])

    for qtr in range(S // QT):
        q0 = qtr * QT
        ctx.dy(lambda e, q0=q0: e.dma_start(
            out=aq_sb.rearrange("p (c t) -> p c t", c=NCH), in_=fmv(d["A0"], q0, q0 + QT)),
            w=[tok_aq])
        for ei in range(NEXP):
            load_we(ei)
            if ei != 8:
                for tb in range(NTBQ):
                    ctx.t(lambda e, ei=ei, tb=tb, q0=q0: e.matmul(
                        pb[5][0:128, 0:TB], e8_sb[0:8, ei * 128:(ei + 1) * 128],
                        ctx.w_em0[0:8, q0 + tb * TB:q0 + (tb + 1) * TB],
                        start=True, stop=True),
                        r=[tok_e8, ctx.tok_wem0], w=[pbt[5]])
                    ctx.s(lambda e, tb=tb: e.activation(
                        out=wb_sb[:, tb * TB:(tb + 1) * TB], in_=pb[5][0:128, 0:TB],
                        func=AF.Copy), r=[pbt[5]], w=[tok_wb])
            for tb in range(NTBQ):
                tc = tb * TB
                for o2 in range(16):
                    bk = o2 % 2
                    for c in range(NCH):
                        ctx.t(lambda e, c=c, o2=o2, tc=tc, bk=bk: e.matmul(
                            pb[bk][:, :], wg_sb[:, c * 2048 + o2 * 128:c * 2048 + (o2 + 1) * 128],
                            aq_sb[:, c * QT + tc:c * QT + tc + TB],
                            start=(c == 0), stop=(c == NCH - 1)),
                            r=[tok_wg, tok_aq], w=[pbt[bk]])
                    ctx.s(lambda e, bk=bk: e.activation(
                        out=gs_sb, in_=pb[bk][:, :], func=AF.Silu), r=[pbt[bk]], w=[tok_gs])
                    for c in range(NCH):
                        ctx.t(lambda e, c=c, o2=o2, tc=tc, bk=bk: e.matmul(
                            pb[2 + bk][:, :], wu_sb[:, c * 2048 + o2 * 128:c * 2048 + (o2 + 1) * 128],
                            aq_sb[:, c * QT + tc:c * QT + tc + TB],
                            start=(c == 0), stop=(c == NCH - 1)),
                            r=[tok_wu, tok_aq], w=[pbt[2 + bk]])
                    ctx.s(lambda e, bk=bk: e.activation(
                        out=us_sb, in_=pb[2 + bk][:, :], func=AF.Copy),
                        r=[pbt[2 + bk]], w=[tok_us])
                    ctx.v(lambda e, o2=o2: e.tensor_mul(
                        p_sb[:, o2 * TB:(o2 + 1) * TB], gs_sb, us_sb),
                        r=[tok_gs, tok_us], w=[tok_p])
                    if ei != 8:
                        ctx.v(lambda e, o2=o2, tb=tb: e.tensor_mul(
                            p_sb[:, o2 * TB:(o2 + 1) * TB], p_sb[:, o2 * TB:(o2 + 1) * TB],
                            wb_sb[:, tb * TB:(tb + 1) * TB]),
                            r=[tok_p, tok_wb], w=[tok_p])
                for o in range(NCH):
                    bk = 4 + o % 2
                    for c2 in range(16):
                        ctx.t(lambda e, c2=c2, o=o, bk=bk: e.matmul(
                            pb[bk][:, :], wd_sb[:, c2 * D + o * 128:c2 * D + (o + 1) * 128],
                            p_sb[:, c2 * TB:(c2 + 1) * TB],
                            start=(c2 == 0), stop=(c2 == 15)),
                            r=[tok_wd, tok_p], w=[pbt[bk]])
                    if ei == 0:
                        ctx.v(lambda e, o=o, tc=tc, bk=bk: e.tensor_copy(
                            out_sb[:, o * QT + tc:o * QT + tc + TB], pb[bk][:, :]),
                            r=[pbt[bk]], w=[tok_out])
                    else:
                        ctx.v(lambda e, o=o, tc=tc, bk=bk: e.scalar_tensor_tensor(
                            out_sb[:, o * QT + tc:o * QT + tc + TB], pb[bk][:, :], 1.0,
                            out_sb[:, o * QT + tc:o * QT + tc + TB], ALU.mult, ALU.add),
                            r=[pbt[bk], tok_out], w=[tok_out])
        for o in range(NCH):
            for tb in range(NTBQ):
                tc = tb * TB
                sl = (o * NTBQ + tb) % 2
                sq, sqt = sq_f[sl]
                xq, xqt = xq_f[sl]
                gt, gtt = gt_f[sl]
                x1, x1t = x1_f[sl]
                ctx.dy(lambda e, sq=sq, o=o, q0=q0, tc=tc: e.dma_start(
                    out=sq, in_=d["S0"][o * 128:(o + 1) * 128, q0 + tc:q0 + tc + TB]),
                    w=[sqt])
                ctx.dy(lambda e, xq=xq, o=o, q0=q0, tc=tc: e.dma_start(
                    out=xq, in_=d["x_fm"][o * 128:(o + 1) * 128, q0 + tc:q0 + tc + TB]),
                    w=[xqt])
                ctx.s(lambda e, gt=gt, o=o, tc=tc: e.activation(
                    out=gt, in_=out_sb[:, o * QT + tc:o * QT + tc + TB], func=AF.Silu),
                    r=[tok_out], w=[gtt])
                ctx.v(lambda e, gt=gt, sq=sq: e.tensor_mul(gt, gt, sq),
                      r=[gtt, sqt], w=[gtt])
                ctx.v(lambda e, x1=x1, xq=xq, gt=gt: e.tensor_add(x1, xq, gt),
                      r=[xqt, gtt], w=[x1t])
                ctx.dy(lambda e, x1=x1, o=o, q0=q0, tc=tc: e.dma_start(
                    out=d["X1"][o * 128:(o + 1) * 128, q0 + tc:q0 + tc + TB], in_=x1),
                    r=[x1t], w=[("d", "X1", qtr, o, tb)])
    ctx.phase()



def emit_ffn1(ctx):
    ab, af, d = ctx.ab, ctx.af, ctx.dram
    pb, pbt = ctx.pb, ctx.pbt
    if "GEMDRAM" in ctx.debug:
        gi = ctx.dram["GEMIN"]
        ctx.das(lambda e: e.dma_start(out=ctx.g_em1[0:16, :], in_=gi), w=[ctx.tok_gem1])

    e16_sb, tok_e16 = ab.alloc(16 * 128)
    ctx.das(lambda e: e.dma_start(out=e16_sb[0:16, :], in_=d["c_E16"]), w=[tok_e16])
    br_sb, tok_br = ab.alloc(D)
    brf_sb, tok_brf = af.alloc(D)
    ctx.das(lambda e: e.dma_start(out=brf_sb[0:16, :], in_=d["f1_br"]), w=[tok_brf])
    ctx.v(lambda e: e.tensor_copy(br_sb[0:16, :], brf_sb[0:16, :]), r=[tok_brf], w=[tok_br])
    ones5_b, tok_o5 = ab.alloc(TB)
    ctx.g(lambda e: e.memset(ones5_b[0:1, :], 1.0), w=[tok_o5])
    bs2_f, tok_bs2 = af.alloc(2048)
    bsh_b, tok_bsh = ab.alloc(1024)
    ctx.das(lambda e: e.dma_start(out=bs2_f[0:1, 0:1024], in_=d["f1_bs"][0:1, :]), w=[tok_bs2])
    ctx.das(lambda e: e.dma_start(out=bs2_f[0:1, 1024:2048], in_=d["f1_bs"][1:2, :]), w=[tok_bs2])
    ctx.v(lambda e: e.tensor_add(bs2_f[0:1, 0:1024], bs2_f[0:1, 0:1024], bs2_f[0:1, 1024:2048]),
          r=[tok_bs2], w=[tok_bs2])
    ctx.v(lambda e: e.tensor_scalar(bsh_b[0:1, :], bs2_f[0:1, 0:1024], 0.5, None, ALU.mult),
          r=[tok_bs2], w=[tok_bsh])

    wr_sb = [ab.alloc(NCH * D) for _ in range(2)]
    wsh_sb, tok_wsh = ab.alloc(NCH * D)
    for i in range(2):
        wb, wt = wr_sb[i]
        ctx.dg(lambda e, wb=wb, i=i: e.dma_start(
            out=wb.rearrange("p (c o) -> p c o", c=NCH),
            in_=d["f1_WsT"][i].rearrange("(c p) o -> p c o", p=128)), w=[wt])
    ctx.v(lambda e: e.tensor_add(wsh_sb, wr_sb[0][0], wr_sb[1][0]),
          r=[wr_sb[0][1], wr_sb[1][1]], w=[tok_wsh])
    ctx.v(lambda e: e.tensor_scalar_mul(wsh_sb, wsh_sb, 0.5), r=[tok_wsh], w=[tok_wsh])
    ctx._wsh_dbg = wsh_sb

    u_sb = [ab.alloc(NCH * TB) for _ in range(3)]
    gb_sb, tok_gb = ab.alloc(16 * TB)
    ue_sb = [ab.alloc(NCH * TB) for _ in range(2)]
    f_f = [af.alloc(TB) for _ in range(2)]
    gt_f = [af.alloc(TB) for _ in range(2)]
    s1_f = [af.alloc(TB) for _ in range(2)]
    x1_f = [af.alloc(TB) for _ in range(2)]
    of_f = [af.alloc(TB) for _ in range(2)]

    def load_wr(ei, slot):
        wb, wt = wr_sb[slot]
        ctx.dg(lambda e, wb=wb, ei=ei: e.dma_start(
            out=wb.rearrange("p (c o) -> p c o", c=NCH),
            in_=d["f1_WrT"][ei].rearrange("(c p) o -> p c o", p=128)), w=[wt])

    for tb in range(NTB):
        t0 = tb * TB
        ub, ut = u_sb[tb % 3]
        ctx.dy(lambda e, ub=ub, t0=t0: e.dma_start(
            out=ub.rearrange("p (c t) -> p c t", c=NCH), in_=fmv(d["A1"], t0, t0 + TB)),
            w=[ut])
        if "UBDBG" in ctx.debug and tb == 0:
            if "UBDBG2" not in ctx.dram:
                ctx.dt("UBDBG2", (128, NCH * TB), BF16, "ExternalOutput")
            ctx.dy(lambda e, ub=ub: e.dma_start(out=ctx.dram["UBDBG2"], in_=ub), r=[ut])
        for ei in range(16):
            bk = ei % 2
            ctx.t(lambda e, ei=ei, t0=t0, bk=bk: e.matmul(
                pb[bk][0:128, :], e16_sb[0:16, ei * 128:(ei + 1) * 128],
                ctx.g_em1[0:16, t0:t0 + TB], start=True, stop=True),
                r=[tok_e16, ctx.tok_gem1], w=[pbt[bk]])
            ctx.s(lambda e, ei=ei, bk=bk: e.activation(
                out=gb_sb[:, ei * TB:(ei + 1) * TB], in_=pb[bk][:, :], func=AF.Copy),
                r=[pbt[bk]], w=[tok_gb])
        if "GBDBG" in ctx.debug and tb == 0:
            if "GBDBG" not in ctx.dram:
                ctx.dt("GBDBG", (128, 16 * TB), BF16, "ExternalOutput")
            ctx.dy(lambda e: e.dma_start(out=ctx.dram["GBDBG"], in_=gb_sb), r=[tok_gb])
        for o in range(NCH):
            for c in range(NCH):
                ctx.t(lambda e, o=o, c=c, ub=ub: e.matmul(
                    pb[o][:, :], wsh_sb[:, c * D + o * 128:c * D + (o + 1) * 128],
                    ub[:, c * TB:(c + 1) * TB], start=(c == 0), stop=False),
                    r=[tok_wsh, ut], w=[pbt[o]])
            ctx.t(lambda e, o=o: e.matmul(
                pb[o][:, :], bsh_b[0:1, o * 128:(o + 1) * 128], ones5_b[0:1, :],
                start=False, stop=False), r=[tok_bsh, tok_o5], w=[pbt[o]])
            ctx.t(lambda e, o=o, t0=t0: e.matmul(
                pb[o][:, :], br_sb[0:16, o * 128:(o + 1) * 128],
                ctx.g_em1[0:16, t0:t0 + TB], start=False, stop=False),
                r=[tok_br, ctx.tok_gem1], w=[pbt[o]])
        if "UBDBG" in ctx.debug and tb == 0:
            if "UBDBG" not in ctx.dram:
                ctx.dt("UBDBG", (128, NCH * TB), BF16, "ExternalOutput")
            ctx.dy(lambda e, ub=ub: e.dma_start(out=ctx.dram["UBDBG"], in_=ub), r=[ut])
        if tb == 0:
            load_wr(0, 0)
        for ei in range(16):
            wb, wt = wr_sb[ei % 2]
            nxt = ei + 1 if ei < 15 else (16 if tb == NTB - 1 else None)
            ueb, uet = ue_sb[ei % 2]
            for c in range(NCH):
                ctx.v(lambda e, ueb=ueb, ub=ub, c=c, ei=ei: e.tensor_mul(
                    ueb[:, c * TB:(c + 1) * TB], ub[:, c * TB:(c + 1) * TB],
                    gb_sb[:, ei * TB:(ei + 1) * TB]), r=[ut, tok_gb], w=[uet])
            for o in range(NCH):
                for c in range(NCH):
                    ctx.t(lambda e, wb=wb, ueb=ueb, o=o, c=c, ei=ei: e.matmul(
                        pb[o][:, :], wb[:, c * D + o * 128:c * D + (o + 1) * 128],
                        ueb[:, c * TB:(c + 1) * TB], start=False,
                        stop=(ei == 15 and c == NCH - 1)),
                        r=[wt, uet], w=[pbt[o]])
            if nxt is not None and nxt < 16:
                load_wr(nxt, nxt % 2)
        if tb < NTB - 1:
            load_wr(0, 0)
        if "UEDBG" in ctx.debug and tb == 0:
            if "UEDBG" not in ctx.dram:
                ctx.dt("UEDBG", (128, NCH * TB), BF16, "ExternalOutput")
            ctx.dy(lambda e, ueb=ue_sb[15 % 2][0]: e.dma_start(
                out=ctx.dram["UEDBG"], in_=ueb), r=[ue_sb[15 % 2][1]])
        for o in range(NCH):
            sl = o % 2
            ff, fft = f_f[sl]
            gt, gtt = gt_f[sl]
            s1, s1t = s1_f[sl]
            x1, x1t = x1_f[sl]
            of, oft = of_f[sl]
            ctx.dy(lambda e, s1=s1, o=o, t0=t0: e.dma_start(
                out=s1, in_=d["S1"][o * 128:(o + 1) * 128, t0:t0 + TB]), w=[s1t])
            ctx.dy(lambda e, x1=x1, o=o, t0=t0: e.dma_start(
                out=x1, in_=d["X1"][o * 128:(o + 1) * 128, t0:t0 + TB]), w=[x1t])
            if "F1PS" in ctx.debug:
                if "F1PS" not in ctx.dram:
                    ctx.dt("F1PS", (D, S), F32, "ExternalOutput")
                ctx.v(lambda e, ff=ff, o=o: e.tensor_copy(ff, pb[o][:, :]),
                      r=[pbt[o]], w=[fft])
                ctx.dy(lambda e, ff=ff, o=o, t0=t0: e.dma_start(
                    out=ctx.dram["F1PS"][o * 128:(o + 1) * 128, t0:t0 + TB], in_=ff),
                    r=[fft])
            ctx.v(lambda e, ff=ff, ub=ub, o=o: e.tensor_add(
                ff, pb[o][:, :], ub[:, o * TB:(o + 1) * TB]), r=[pbt[o], ut], w=[fft])
            if "F1DBG" in ctx.debug:
                if "F1DBG" not in ctx.dram:
                    ctx.dt("F1DBG", (D, S), F32, "ExternalOutput")
                ctx.dy(lambda e, ff=ff, o=o, t0=t0: e.dma_start(
                    out=ctx.dram["F1DBG"][o * 128:(o + 1) * 128, t0:t0 + TB], in_=ff),
                    r=[fft])
            ctx.s(lambda e, gt=gt, ff=ff: e.activation(out=gt, in_=ff, func=AF.Silu),
                  r=[fft], w=[gtt])
            ctx.v(lambda e, gt=gt, s1=s1: e.tensor_mul(gt, gt, s1), r=[gtt, s1t], w=[gtt])
            ctx.v(lambda e, of=of, x1=x1, gt=gt: e.tensor_add(of, x1, gt),
                  r=[x1t, gtt], w=[oft])
            ctx.dy(lambda e, of=of, o=o, t0=t0: e.dma_start(
                out=d["OUT"][o * 128:(o + 1) * 128, t0:t0 + TB], in_=of),
                r=[oft], w=[("d", "OUT", tb, o)])
    ctx.phase()



SCRATCH = {
    "A0": ((D, S), BF16), "Q0": ((D, S), BF16), "K0": ((D, S), BF16),
    "V0": ((S, D), BF16), "S0": ((D, S), F32), "X1": ((D, S), F32),
    "A1": ((D, S), BF16), "Q1": ((D, S), BF16), "K1": ((D, S), BF16),
    "V1": ((S, D), BF16), "S1": ((D, S), F32),
}

PHASES_ALL = ("ln0", "proj0", "attn0", "ffn0", "ln1", "proj1", "attn1", "ffn1")


def build_nc(in_map, phases=PHASES_ALL, dbg=(), dumps=(), dbgset=()):
    stack = ExitStack()
    nc = bass.Bass("TRN2", target_bir_lowering=False)
    ctx = Ctx(nc, stack)
    ctx.debug = set(dbgset)
    _mk(ctx)
    np2dt = {np.dtype(np.float32): F32, BFNP: BF16}
    for name, arr in in_map.items():
        ctx.dt(name, arr.shape, np2dt[arr.dtype], "ExternalInput")
    for name, (shape, dt_) in SCRATCH.items():
        if name in ctx.dram:
            continue
        ctx.dt(name, shape, dt_, "ExternalOutput" if name in dbg else "Internal")
    ctx.dt("OUT", (D, S), F32, "ExternalOutput")
    emit_init(ctx)
    for ph in phases:
        if ph == "ln0":
            emit_ln_gate(ctx, 0)
        elif ph == "proj0":
            emit_proj(ctx, 0)
        elif ph == "attn0":
            emit_attn(ctx, 0)
        elif ph == "ffn0":
            emit_ffn0(ctx)
        elif ph == "ln1":
            emit_ln_gate(ctx, 1)
        elif ph == "proj1":
            emit_proj(ctx, 1)
        elif ph == "attn1":
            emit_attn(ctx, 1)
        elif ph == "ffn1":
            emit_ffn1(ctx)
        else:
            raise ValueError(ph)
    for nm, getap, shape, dt_ in dumps:
        dd = ctx.dt(nm, shape, dt_, "ExternalOutput")
        ctx.dy(lambda e, dd=dd, ap=getap(ctx): e.dma_start(out=dd, in_=ap))
    ctx.phase()
    ctx.sc.flush()
    return nc, stack


def kernel(**inputs):
    consts = host_consts()
    in_maps = [prep_core_inputs(b, inputs, consts) for b in range(8)]
    nc, stack = build_nc(in_maps[0])
    try:
        res = run_bass_kernel_spmd(nc, in_maps, core_ids=list(range(8)))
    finally:
        stack.close()
    return np.stack([np.ascontiguousarray(r["OUT"].T) for r in res.results], axis=0)
